# version 18
# baseline (speedup 1.0000x reference)
import numpy as np
import concourse.bass as bass
import concourse.mybir as mybir
from concourse.bass_utils import run_bass_kernel_spmd

F32 = mybir.dt.float32
BF16 = mybir.dt.bfloat16
AF = mybir.ActivationFunctionType
ALU = mybir.AluOpType

NCORES = 8
D = 2048
HID = 8192
TOK = 2048
T = 128
NC = TOK // T
W = 512
CPW = W // T
NW = TOK // W
NHT = HID // 128
NFT = D // 128
EPS = 1e-5
NJ = 16


class Plan:

    def __init__(self):
        self.ops = {"T": [], "V": [], "A": [], "G": [], "S": []}
        self.counts = {}

    def op(self, eng, fn, deps=(), inc=None):
        waits = {}
        for d in deps:
            if d is None:
                continue
            s, v = d
            waits[s] = max(waits.get(s, 0), v)
        tok = None
        if inc is not None:
            amt = 16 if inc.startswith("D") else 1
            self.counts[inc] = self.counts.get(inc, 0) + amt
            tok = (inc, self.counts[inc])
        self.ops[eng].append((fn, sorted(waits.items()), inc))
        return tok


DBG_SKIP = set()
DBG_X2 = False


def build_program(max_slot=NC):
    nc = bass.Bass("TRN2", target_bir_lowering=False)

    xH = nc.dram_tensor("xH", [TOK, D], F32, kind="ExternalInput").ap()
    wgT = nc.dram_tensor("wgH", [HID, D], F32, kind="ExternalInput").ap()
    wuT = nc.dram_tensor("wuH", [HID, D], F32, kind="ExternalInput").ap()
    wdT = nc.dram_tensor("wdH", [HID, D], F32, kind="ExternalInput").ap()
    wqkv_d = nc.dram_tensor("wqkv", [128, 64 * 128], F32, kind="ExternalInput").ap()
    wout_d = nc.dram_tensor("wout4", [128, 128], F32, kind="ExternalInput").ap()
    bq_d = nc.dram_tensor("bq", [128, 16], F32, kind="ExternalInput").ap()
    bk_d = nc.dram_tensor("bk", [128, 32], F32, kind="ExternalInput").ap()
    bv_d = nc.dram_tensor("bv", [128, 16], F32, kind="ExternalInput").ap()
    bo_d = nc.dram_tensor("bo", [128, 1], F32, kind="ExternalInput").ap()
    x2h = nc.dram_tensor("x2h", [D, TOK], BF16, kind="ExternalOutput" if DBG_X2 else "Internal").ap()
    yT = nc.dram_tensor("yT", [D, TOK], F32, kind="ExternalOutput").ap()

    A_ = nc.alloc_sbuf_tensor
    wqkv_sb = A_("wqkv_sb", [128, 64 * 128], BF16)
    wout_sb = A_("wout_sb", [128, 128], BF16)
    bq_sb = A_("bq_sb", [128, 16], F32)
    bk_sb = A_("bk_sb", [128, 32], F32)
    bv_sb = A_("bv_sb", [128, 16], F32)
    bo_sb = A_("bo_sb", [128, 1], F32)
    ones_c = A_("ones_c", [128, 1], BF16)
    ones_r = A_("ones_r", [1, 128], BF16)
    cst_sb = A_("cst_sb", [128, 2], F32)
    nc.const_aps.register(F32, EPS, cst_sb.ap()[:, 0:1])
    nc.const_aps.register(F32, 1.0, cst_sb.ap()[:, 1:2])

    wg_sb = [A_(f"wg_sb{i}", [128, NFT * 128], BF16) for i in range(2)]
    wu_sb = [A_(f"wu_sb{i}", [128, NFT * 128], BF16) for i in range(2)]
    wd_sb = [A_(f"wd_sb{i}", [128, 16 * 128], BF16) for i in range(2)]

    xb_sb = [A_(f"xb_sb{i}", [128, NFT * T], BF16) for i in range(3)]
    sq_sb = A_("sq_sb", [128, NFT * T], BF16)
    xn1_sb = A_("xn1_sb", [128, NFT * T], BF16)
    es_sb = [A_(f"es_sb{i}", [128, 32 * T], BF16) for i in range(2)]
    q_sb = A_("q_sb", [128, NJ * T], BF16)
    v_sb = A_("v_sb", [128, NJ * T], BF16)
    kd_sb = A_("kd_sb", [128, 32 * T], BF16)
    u_sb = A_("u_sb", [128, T], BF16)
    zs_sb = A_("zs_sb", [128, 16 * T], BF16)
    zf_sb = [A_(f"zf_sb{i}", [128, T], BF16) for i in range(2)]
    rz_sb = [A_(f"rz_sb{i}", [128, T], BF16) for i in range(2)]
    ctx_sb = A_("ctx_sb", [128, 32 * T], BF16)
    ao_sb = A_("ao_sb", [128, NFT * T], BF16)
    x2c_sb = A_("x2c_sb", [128, NFT * T], BF16)
    ab1_sb = [A_(f"ab1_sb{i}", [128, 2 * T], BF16) for i in range(2)]
    ab3_sb = A_("ab3_sb", [128, 2 * T], BF16)
    st_sb = A_("st_sb", [1, 3 * T], F32)
    ab_row = A_("ab_row", [1, 2 * T], BF16)
    a2b2_w = A_("a2b2_w", [1, 2 * W], BF16)
    st4_sb = A_("st4_sb", [1, 3 * W], F32)
    ab4_row = A_("ab4_row", [1, 2 * W], BF16)

    xn2_sb = A_("xn2_sb", [128, NFT * W], BF16)
    gated_sb = A_("gated_sb", [128, 32 * W], BF16)
    y2_sb = A_("y2_sb", [128, NFT * W], BF16)
    a2bc_sb = A_("a2bc_sb", [128, W], BF16)
    b2bc_sb = A_("b2bc_sb", [128, W], BF16)
    a4bc_sb = A_("a4bc_sb", [128, W], BF16)
    b4bc_sb = A_("b4bc_sb", [128, W], BF16)
    e1_sb = A_("e1_sb", [128, W], F32)
    m_sb = [A_(f"m_sb{i}", [128, W], BF16) for i in range(2)]
    t1_sb = [A_(f"t1_sb{i}", [128, W], BF16) for i in range(2)]
    sqw_sb = A_("sqw_sb", [128, W], BF16)
    x2s_sb = [A_(f"x2s_sb{i}", [128, W], BF16) for i in range(2)]
    x2scr_sb = [A_("x2scr0", [128, W], BF16)] * 2
    outs_sb = A_("outs_sb", [128, W], F32)

    P_ = nc.alloc_psum_tensor
    gps = [P_(f"gps{i}", [128, W], F32) for i in range(2)]
    ups = [P_(f"ups{i}", [128, W], F32) for i in range(2)]
    dps = [P_(f"dps{i}", [128, W], F32) for i in range(2)]
    mps = [P_(f"mps{i}", [128, 512], F32) for i in range(2)]

    plan = Plan()
    last = {}

    def dep(*keys):
        return [last.get(k) for k in keys]

    mbank_i = [0]

    def next_mbank():
        b = mbank_i[0] % 2
        mbank_i[0] += 1
        return b

    t = plan.op("G", lambda g: g.dma_start(out=wqkv_sb.ap(), in_=wqkv_d), inc="Dcq")
    last["wqkv"] = t
    t = plan.op("G", lambda g: g.dma_start(out=wout_sb.ap(), in_=wout_d), inc="Dco")
    last["wout"] = t
    for name, sb, dr in (("bq", bq_sb, bq_d), ("bk", bk_sb, bk_d),
                         ("bv", bv_sb, bv_d), ("bo", bo_sb, bo_d)):
        t = plan.op("S", lambda s, sb=sb, dr=dr: s.dma_start(out=sb.ap(), in_=dr),
                    inc="Db")
    last["bias"] = t
    t = plan.op("V", lambda v: v.memset(ones_c.ap(), 1.0), inc=None)
    t = plan.op("V", lambda v: v.memset(ones_r.ap(), 1.0), inc="V")
    last["ones"] = t
    t = plan.op("V", lambda v: v.memset(cst_sb.ap()[:, 0:1], EPS), inc=None)
    t = plan.op("V", lambda v: v.memset(cst_sb.ap()[:, 1:2], 1.0), inc="V")
    last["consts"] = t

    def dma_xb(c):
        buf = xb_sb[c % 3]

        def fn(g, c=c, buf=buf):
            return g.dma_start(out=buf.ap(),
                               in_=xH[c * 128:(c + 1) * 128, :])
        last[f"xb{c}"] = plan.op("G", fn, deps=dep(f"x2_{c - 3}"),
                                 inc=f"Dxb{c % 3}")

    def ln_stats_mms(src_sb, src_tok, sq_tok, extra=()):
        b = next_mbank()
        tk = None
        for ft in range(NFT):
            def fn(tn, ft=ft, b=b, src_sb=src_sb):
                return tn.matmul(out=mps[b].ap()[0:1, 0:T], lhsT=ones_c.ap(),
                                 rhs=src_sb.ap()[:, ft * T:(ft + 1) * T],
                                 start=(ft == 0), stop=(ft == NFT - 1),
                                 skip_group_check=True)
            d = (dep(f"pm{b}", "ones") + [src_tok] + list(extra)) if ft == 0 else ()
            tk = plan.op("T", fn, deps=d, inc=None)
        for ft in range(NFT):
            def fn(tn, ft=ft, b=b):
                return tn.matmul(out=mps[b].ap()[0:1, T:2 * T], lhsT=ones_c.ap(),
                                 rhs=sq_sb.ap()[:, ft * T:(ft + 1) * T],
                                 start=(ft == 0), stop=(ft == NFT - 1),
                                 skip_group_check=True)
            tk = plan.op("T", fn, deps=[sq_tok] if ft == 0 else (),
                         inc="T" if ft == NFT - 1 else None)
        return b, tk

    def ln_stats_math(bank, mm_tok, n=D):
        inv = 1.0 / n
        s0 = st_sb.ap()[0:1, 0:T]
        s1 = st_sb.ap()[0:1, T:2 * T]
        s2 = st_sb.ap()[0:1, 2 * T:3 * T]
        tm = plan.op("V", lambda v: v.tensor_scalar_mul(
            out=s0, in0=mps[bank].ap()[0:1, 0:T], scalar1=inv),
            deps=[mm_tok] + dep("st_free"), inc="V")
        tvv = plan.op("V", lambda v: v.tensor_scalar_mul(
            out=s1, in0=mps[bank].ap()[0:1, T:2 * T], scalar1=inv), inc="V")
        tmsq = plan.op("V", lambda v: v.tensor_mul(out=s2, in0=s0, in1=s0),
                       deps=[tm], inc="V")
        t = plan.op("V", lambda v: v.tensor_sub(out=s1, in0=s1, in1=s2),
                    deps=[tvv, tmsq], inc="V")
        last[f"pm{bank}"] = t
        ta = plan.op("A", lambda a: a.activation(
            out=s1, in_=s1, func=AF.Ln, bias=EPS),
            deps=[t] + dep("consts"), inc="A")
        ta = plan.op("A", lambda a: a.activation(
            out=s1, in_=s1, func=AF.Exp, scale=-0.5), deps=[ta], inc="A")
        tb = plan.op("V", lambda v: v.tensor_mul(out=s2, in0=s0, in1=s1),
                     deps=[ta, tm] + dep("abrow_free"), inc="V")
        t = plan.op("V", lambda v: v.tensor_copy(
            out=ab_row.ap()[0:1, 0:T], in_=s1), deps=[ta], inc="V")
        t = plan.op("V", lambda v: v.tensor_copy(
            out=ab_row.ap()[0:1, T:2 * T], in_=s2), deps=[tb], inc="V")
        last["st_free"] = t
        return t

    def ln_bcast(row_tok, dst_sb, free_key):
        b = next_mbank()
        tk = plan.op("T", lambda tn, b=b: tn.matmul(
            out=mps[b].ap()[:, 0:2 * T], lhsT=ones_r.ap(), rhs=ab_row.ap(),
            start=True, stop=True), deps=dep(f"pm{b}", "ones") + [row_tok],
            inc="T")
        te = plan.op("A", lambda a, b=b, dst_sb=dst_sb: a.activation(
            out=dst_sb.ap(), in_=mps[b].ap()[:, 0:2 * T], func=AF.Copy),
            deps=[tk] + dep(free_key), inc="A")
        last[f"pm{b}"] = te
        last["abrow_free"] = te
        return te

    def qkv_chunk(c):
        xn1_tok = last["xn1"]
        te = None
        for grp in range([] if "qkv_k" in DBG_SKIP else range(8)) if False else (range(0) if "qkv_k" in DBG_SKIP else range(8)):
            b = next_mbank()
            bank_dep = dep(f"pm{b}", "wqkv") + [xn1_tok]
            tks = []
            for i in range(4):
                e = grp * 4 + i
                def fn(tn, e=e, i=i, b=b):
                    return tn.matmul(
                        out=mps[b].ap()[:, i * T:(i + 1) * T],
                        lhsT=wqkv_sb.ap()[:, 2048 + e * 128:2048 + (e + 1) * 128],
                        rhs=xn1_sb.ap()[:, (e // 2) * T:(e // 2 + 1) * T],
                        start=True, stop=True)
                tks.append(plan.op("T", fn, deps=bank_dep if i == 0 else (),
                                   inc="T"))
            for i in range(4):
                e = grp * 4 + i
                d = [tks[3], last["bias"]]
                if grp == 0 and i == 0:
                    d += dep("kd_free")
                def fe(a, e=e, i=i, b=b):
                    return a.activation(out=kd_sb.ap()[:, e * T:(e + 1) * T],
                                        in_=mps[b].ap()[:, i * T:(i + 1) * T],
                                        func=AF.Identity,
                                        bias=bk_sb.ap()[:, e:e + 1])
                if "qkv_ev" not in DBG_SKIP:
                    te = plan.op("A", fe, deps=d, inc="A" if i == 3 else None)
                else:
                    te = tks[i]
            last[f"pm{b}"] = te
        last["kd"] = te
        for kind, dst, bias, coff in (("q", q_sb, bq_sb, 0),
                                      ("v", v_sb, bv_sb, 6144)):
            if f"qkv_{kind}" in DBG_SKIP:
                continue
            for grp in range(4):
                b = next_mbank()
                bank_dep = dep(f"pm{b}", "wqkv") + [xn1_tok]
                tks = []
                for i in range(4):
                    cp = grp * 4 + i
                    def fn(tn, cp=cp, i=i, b=b, coff=coff):
                        return tn.matmul(
                            out=mps[b].ap()[:, i * T:(i + 1) * T],
                            lhsT=wqkv_sb.ap()[:, coff + cp * 128:
                                              coff + (cp + 1) * 128],
                            rhs=xn1_sb.ap()[:, cp * T:(cp + 1) * T],
                            start=True, stop=True)
                    tks.append(plan.op("T", fn,
                                       deps=bank_dep if i == 0 else (),
                                       inc="T"))
                te = None
                for i in range(4):
                    cp = grp * 4 + i
                    d = [tks[3]]
                    if grp == 0 and i == 0:
                        d += dep(f"{kind}_free")
                    def fe(a, cp=cp, i=i, b=b, dst=dst, bias=bias):
                        return a.activation(out=dst.ap()[:, cp * T:(cp + 1) * T],
                                            in_=mps[b].ap()[:, i * T:(i + 1) * T],
                                            func=AF.Identity,
                                            bias=bias.ap()[:, cp:cp + 1])
                    if "qkv_ev" not in DBG_SKIP:
                        te = plan.op("A", fe, deps=d, inc="A" if i == 3 else None)
                    else:
                        te = tks[i]
                last[f"pm{b}"] = te
            last[kind] = te

    def emit_uP(c, j):
        buf = es_sb[j % 2]
        tu = plan.op("V", lambda v, j=j: v.tensor_mul(
            out=u_sb.ap(), in0=v_sb.ap()[:, j * T:(j + 1) * T],
            in1=rz_sb[j % 2].ap()),
            deps=[last[f"rz_{c}_{j}"], last["v"]] + dep("u_free"), inc="V")
        last[f"rzfree{j % 2}"] = tu
        if j == NJ - 1:
            last["v_free"] = tu
        ub = u_sb.ap().unsqueeze(1).broadcast_to([128, 32, T])
        s3 = buf.ap().rearrange("p (e t) -> p e t", e=32)
        tp = plan.op("V", lambda v, s3=s3, ub=ub: v.tensor_tensor(
            out=s3, in0=s3, in1=ub, op=ALU.mult), deps=[tu], inc="V")
        last["u_free"] = tp
        eng = "V"
        if j == 0:
            tg = plan.op(eng, lambda g, buf=buf: g.tensor_copy(
                out=ctx_sb.ap(), in_=buf.ap()),
                deps=[tp] + dep(f"ctxdone_{c - 1}"), inc=eng)
        else:
            tg = plan.op(eng, lambda g, buf=buf: g.tensor_add(
                out=ctx_sb.ap(), in0=ctx_sb.ap(), in1=buf.ap()),
                deps=[tp, last[f"ctxadd_{c}_{j - 1}"]], inc=eng)
        last[f"ctxadd_{c}_{j}"] = tg
        if j == NJ - 1:
            last[f"ctx_{c}"] = tg

    def emit_gu(w, h):
        pb = h % 2
        def d_g(g, h=h, pb=pb):
            return g.dma_start(out=wg_sb[pb].ap(),
                               in_=wgT[h * 128:(h + 1) * 128, :])
        tg = plan.op("G", d_g, deps=dep(f"wgfree{pb}"), inc=f"Dwg{pb}")
        def d_u(g, h=h, pb=pb):
            return g.dma_start(out=wu_sb[pb].ap(),
                               in_=wuT[h * 128:(h + 1) * 128, :])
        tu_d = plan.op("G", d_u, deps=dep(f"wufree{pb}"), inc=f"Dwu{pb}")
        first = [tg, last[f"xn2_{w}"]] + dep(f"gufree{pb}")
        tk = None
        for ft in range(NFT):
            def fn(tn, ft=ft, pb=pb):
                return tn.matmul(out=gps[pb].ap(),
                                 lhsT=wg_sb[pb].ap()[:, ft * 128:(ft + 1) * 128],
                                 rhs=xn2_sb.ap()[:, ft * W:(ft + 1) * W],
                                 start=(ft == 0), stop=(ft == NFT - 1))
            tk = plan.op("T", fn, deps=first if ft == 0 else (),
                         inc="T" if ft == NFT - 1 else None)
        tgate = tk
        last[f"wgfree{pb}"] = tgate
        for ft in range(NFT):
            def fn(tn, ft=ft, pb=pb):
                return tn.matmul(out=ups[pb].ap(),
                                 lhsT=wu_sb[pb].ap()[:, ft * 128:(ft + 1) * 128],
                                 rhs=xn2_sb.ap()[:, ft * W:(ft + 1) * W],
                                 start=(ft == 0), stop=(ft == NFT - 1))
            tk = plan.op("T", fn, deps=[tu_d] if ft == 0 else (),
                         inc="T" if ft == NFT - 1 else None)
        tup = tk
        last[f"wufree{pb}"] = tup
        if h == NHT - 1:
            last["xn2_free"] = tup
        ta = plan.op("A", lambda a, pb=pb: a.activation(
            out=e1_sb.ap(), in_=gps[pb].ap(), func=AF.Exp, scale=-1.0),
            deps=[tgate] + dep("e1free"), inc="A")
        ta = plan.op("A", lambda a: a.activation(
            out=e1_sb.ap(), in_=e1_sb.ap(), func=AF.Ln, bias=1.0),
            deps=[ta] + dep("consts"), inc="A")
        ta = plan.op("A", lambda a, pb=pb: a.activation(
            out=m_sb[pb].ap(), in_=e1_sb.ap(), func=AF.Exp, scale=-1.0),
            deps=[ta] + dep(f"mfree{pb}"), inc="A")
        last["e1free"] = ta
        tv = plan.op("V", lambda v, pb=pb: v.tensor_mul(
            out=t1_sb[pb].ap(), in0=m_sb[pb].ap(), in1=gps[pb].ap()),
            deps=[ta, tgate] + dep(f"t1free{pb}"), inc="V")
        last[f"mfree{pb}"] = tv
        hs = h % 32
        tv = plan.op("V", lambda v, hs=hs, pb=pb: v.tensor_mul(
            out=gated_sb.ap()[:, hs * W:(hs + 1) * W],
            in0=t1_sb[pb].ap(), in1=ups[pb].ap()),
            deps=[tup, tv] + dep(f"gatedfree{hs}"), inc="V")
        last[f"t1free{pb}"] = tv
        last[f"gufree{pb}"] = tv
        last[f"gated{hs}"] = tv

    def emit_down(w, hh, dt):
        pb = dt % 2
        tk = None
        for half in (0, 1):
            wb = (2 * dt + half) % 2
            def d_w(g, hh=hh, dt=dt, half=half, wb=wb):
                r0 = (((2 * hh + half) * 16 + dt)) * 128
                return g.dma_start(out=wd_sb[wb].ap(),
                                   in_=wdT[r0:r0 + 128, :])
            td = plan.op("G", d_w, deps=dep(f"wdfree{wb}"),
                         inc=f"Dwd{wb}")
            first = [td]
            if half == 0:
                first += [last.get(f"dfree{pb}")] + \
                    [last[f"gated{i}"] for i in range(32)]
            for sh in range(16):
                ht = 16 * half + sh
                def fn(tn, ht=ht, sh=sh, pb=pb, wb=wb, half=half):
                    return tn.matmul(
                        out=dps[pb].ap(),
                        lhsT=wd_sb[wb].ap()[:, sh * 128:(sh + 1) * 128],
                        rhs=gated_sb.ap()[:, ht * W:(ht + 1) * W],
                        start=(ht == 0), stop=(ht == 31))
                tk = plan.op("T", fn, deps=first if sh == 0 else (),
                             inc="T" if (sh == 15) else None)
            last[f"wdfree{wb}"] = tk
        if dt == 15:
            for i in range(32):
                last[f"gatedfree{i}"] = tk
        if hh == 0:
            te = plan.op("A", lambda a, dt=dt, pb=pb: a.activation(
                out=y2_sb.ap()[:, dt * W:(dt + 1) * W], in_=dps[pb].ap(),
                func=AF.Copy), deps=[tk] + dep(f"y2free{dt}"), inc="A")
        else:
            te = plan.op("V", lambda v, dt=dt, pb=pb: v.tensor_add(
                out=y2_sb.ap()[:, dt * W:(dt + 1) * W],
                in0=y2_sb.ap()[:, dt * W:(dt + 1) * W], in1=dps[pb].ap()),
                deps=[tk], inc="V")
            last[f"y2_{dt}"] = te
        last[f"dfree{pb}"] = te

    def emit_ln4(w):
        y2deps = [last[f"y2_{dt}"] for dt in range(NFT)]
        b1 = next_mbank()
        tk = None
        for dt in range(NFT):
            def fn(tn, dt=dt, b1=b1):
                return tn.matmul(out=mps[b1].ap()[0:1, 0:W], lhsT=ones_c.ap(),
                                 rhs=y2_sb.ap()[:, dt * W:(dt + 1) * W],
                                 start=(dt == 0), stop=(dt == NFT - 1),
                                 skip_group_check=True)
            d = (dep(f"pm{b1}", "ones") + y2deps) if dt == 0 else ()
            tk = plan.op("T", fn, deps=d, inc="T" if dt == NFT - 1 else None)
        sum_tok = tk
        b2 = next_mbank()
        tprev = None
        tsq = None
        for dt in range(NFT):
            d = [tprev, tsq] if tprev is not None else (
                y2deps + dep(f"pm{b2}"))
            tsq = plan.op("A", lambda a, dt=dt: a.activation(
                out=sqw_sb.ap(), in_=y2_sb.ap()[:, dt * W:(dt + 1) * W],
                func=AF.Square), deps=d, inc="A")
            def fn(tn, dt=dt, b2=b2):
                return tn.matmul(out=mps[b2].ap()[0:1, 0:W], lhsT=ones_c.ap(),
                                 rhs=sqw_sb.ap(), start=(dt == 0),
                                 stop=(dt == NFT - 1), skip_group_check=True)
            tprev = plan.op("T", fn, deps=[tsq], inc="T")
        sq_tok = tprev
        inv = 1.0 / D
        s0 = st4_sb.ap()[0:1, 0:W]
        s1 = st4_sb.ap()[0:1, W:2 * W]
        s2 = st4_sb.ap()[0:1, 2 * W:3 * W]
        tm = plan.op("V", lambda v, b1=b1: v.tensor_scalar_mul(
            out=s0, in0=mps[b1].ap()[0:1, 0:W], scalar1=inv),
            deps=[sum_tok, sq_tok], inc="V")
        tvv = plan.op("V", lambda v, b2=b2: v.tensor_scalar_mul(
            out=s1, in0=mps[b2].ap()[0:1, 0:W], scalar1=inv), inc="V")
        tmsq = plan.op("V", lambda v: v.tensor_mul(out=s2, in0=s0, in1=s0),
                       deps=[tm], inc="V")
        t = plan.op("V", lambda v: v.tensor_sub(out=s1, in0=s1, in1=s2),
                    deps=[tvv, tmsq], inc="V")
        last[f"pm{b1}"] = t
        last[f"pm{b2}"] = t
        ta = plan.op("A", lambda a: a.activation(
            out=s1, in_=s1, func=AF.Ln, bias=EPS),
            deps=[t] + dep("consts"), inc="A")
        ta = plan.op("A", lambda a: a.activation(
            out=s1, in_=s1, func=AF.Exp, scale=-0.5), deps=[ta], inc="A")
        tb = plan.op("V", lambda v: v.tensor_mul(out=s2, in0=s0, in1=s1),
                     deps=[ta, tm], inc="V")
        t = plan.op("V", lambda v: v.tensor_copy(
            out=ab4_row.ap()[0:1, 0:W], in_=s1), deps=[ta], inc="V")
        t = plan.op("V", lambda v: v.tensor_copy(
            out=ab4_row.ap()[0:1, W:2 * W], in_=s2), deps=[tb], inc="V")
        last["ab4row"] = t

    def emit_bc4(w):
        te = None
        for half, dst in ((0, a4bc_sb), (1, b4bc_sb)):
            b = next_mbank()
            tk = plan.op("T", lambda tn, half=half, b=b: tn.matmul(
                out=mps[b].ap()[:, 0:W], lhsT=ones_r.ap(),
                rhs=ab4_row.ap()[0:1, half * W:(half + 1) * W],
                start=True, stop=True), deps=dep(f"pm{b}") + [last["ab4row"]],
                inc="T")
            te = plan.op("A", lambda a, b=b, dst=dst: a.activation(
                out=dst.ap(), in_=mps[b].ap()[:, 0:W], func=AF.Copy),
                deps=[tk] + dep("ab4bcfree"), inc="A")
            last[f"pm{b}"] = te
        last["ab4bc"] = te

    def emit_fin(w, k):
        pb = k % 2
        def d_r(s, w=w, k=k, pb=pb):
            src = x2h.rearrange("(fp p) tok -> p fp tok", p=128)[
                :, k:k + 1, w * W:(w + 1) * W]
            return s.dma_start(out=x2s_sb[pb].ap().unsqueeze(1), in_=src)
        tr = plan.op("S", d_r,
                     deps=dep(f"x2sfree{pb}") + [last[f"spill_w{w}"]],
                     inc=f"Dr{pb}")
        y2r = y2_sb.ap()[:, k * W:(k + 1) * W]
        scr = x2scr_sb[pb].ap()
        tv = plan.op("V", lambda v, y2r=y2r, scr=scr: v.tensor_mul(
            out=scr, in0=y2r, in1=a4bc_sb.ap()),
            deps=[last["ab4bc"], last[f"y2_{k}"]] + dep("x2scrfree"),
            inc="V")
        tv = plan.op("V", lambda v, scr=scr: v.tensor_sub(
            out=scr, in0=scr, in1=b4bc_sb.ap()), deps=[tv], inc="V")
        tv = plan.op("V", lambda v, scr=scr, pb=pb: v.tensor_add(
            out=outs_sb.ap(), in0=x2s_sb[pb].ap(), in1=scr),
            deps=[tv, tr] + dep("outsfree"), inc="V")
        last[f"x2sfree{pb}"] = tv
        last["x2scrfree"] = tv
        last[f"y2free{k}"] = tv
        if k == NFT - 1:
            last["ab4bcfree"] = tv
        def d_wr(s, w=w, k=k):
            dst = yT.rearrange("(fp p) tok -> p fp tok", p=128)[
                :, k:k + 1, w * W:(w + 1) * W]
            return s.dma_start(out=dst, in_=outs_sb.ap().unsqueeze(1))
        tw = plan.op("S", d_wr, deps=[tv], inc="Do")
        last["outsfree"] = tw

    def emit_xn2prep(w):
        te = None
        for half, dst in ((0, a2bc_sb), (1, b2bc_sb)):
            b = next_mbank()
            tk = plan.op("T", lambda tn, half=half, b=b: tn.matmul(
                out=mps[b].ap()[:, 0:W], lhsT=ones_r.ap(),
                rhs=a2b2_w.ap()[0:1, half * W:(half + 1) * W],
                start=True, stop=True),
                deps=dep(f"pm{b}") + [last[f"a2b2_{w}"]], inc="T")
            te = plan.op("A", lambda a, b=b, dst=dst: a.activation(
                out=dst.ap(), in_=mps[b].ap()[:, 0:W], func=AF.Copy),
                deps=[tk] + dep("a2bcfree"), inc="A")
            last[f"pm{b}"] = te
        ab2bc = te
        last["a2b2free"] = te
        trs = []
        for k in range(NFT):
            def d_r(s, w=w, k=k):
                src = x2h.rearrange("(fp p) tok -> p fp tok", p=128)[
                    :, k:k + 1, w * W:(w + 1) * W]
                return s.dma_start(
                    out=xn2_sb.ap()[:, k * W:(k + 1) * W].unsqueeze(1),
                    in_=src)
            d = dep("xn2_free") + [last[f"spill_w{w}"]]
            if k >= 4:
                d.append(trs[k - 4])
            trs.append(plan.op("S", d_r, deps=d, inc=f"Dxr{k % 4}"))
        tv = None
        for k in range(NFT):
            xn2r = xn2_sb.ap()[:, k * W:(k + 1) * W]
            tv = plan.op("V", lambda v, xn2r=xn2r: v.tensor_mul(
                out=xn2r, in0=xn2r, in1=a2bc_sb.ap()),
                deps=[trs[k], ab2bc], inc="V")
            tv = plan.op("V", lambda v, xn2r=xn2r: v.tensor_sub(
                out=xn2r, in0=xn2r, in1=b2bc_sb.ap()), deps=[tv], inc="V")
        last["a2bcfree"] = tv
        last[f"xn2_{w}"] = tv

    dma_xb(0)
    dma_xb(1)
    dma_xb(2)

    for c in range(-1, min(max_slot, NC) + 1):
        pw = (c // CPW) - 1 if 0 <= c < NC else (NW - 1 if c == NC else -1)
        atoms = []
        if 0 <= c < NC and pw >= 0:
            al = ATOMS[pw]
            n = (len(al) + CPW - 1) // CPW
            cl = c % CPW
            atoms = al[cl * n:(cl + 1) * n]
        elif c == NC:
            atoms = ATOMS[NW - 1]
        ai = [0]

        def pull(k):
            while k > 0 and ai[0] < len(atoms):
                a = atoms[ai[0]]
                ai[0] += 1
                k -= 1
                if a[0] == "gu":
                    emit_gu(a[1], a[2])
                elif a[0] == "down":
                    emit_down(a[1], a[2], a[3])
                elif a[0] == "ln4":
                    emit_ln4(a[1])
                elif a[0] == "bc4":
                    emit_bc4(a[1])
                elif a[0] == "fin":
                    emit_fin(a[1], a[2])

        if 0 <= c - 1 < NC:
            cp = c - 1
            ctx_tok = last[f"ctx_{cp}"]
            te = None
            for grp in range(4):
                b = next_mbank()
                bank_dep = dep(f"pm{b}", "wout") + [ctx_tok]
                tk = None
                fst = True
                for i in range(4):
                    mp = grp * 4 + i
                    for par in (0, 1):
                        m = 2 * mp + par
                        def fn(tn, m=m, i=i, b=b, par=par):
                            return tn.matmul(
                                out=mps[b].ap()[64 * par:64 * par + 64,
                                                i * T:(i + 1) * T],
                                lhsT=wout_sb.ap()[:, 64 * par:64 * par + 64],
                                rhs=ctx_sb.ap()[:, m * T:(m + 1) * T],
                                start=True, stop=True,
                                tile_position=(0, 64 * par),
                                skip_group_check=True)
                        tk = plan.op("T", fn, deps=bank_dep if fst else (),
                                     inc="T" if (i == 3 and par == 1) else None)
                        fst = False
                def fe(a, grp=grp, b=b):
                    return a.activation(
                        out=ao_sb.ap()[:, grp * 4 * T:(grp + 1) * 4 * T],
                        in_=mps[b].ap()[:, 0:4 * T], func=AF.Identity,
                        bias=bo_sb.ap()[:, 0:1])
                d = [tk] + (dep("ao_free") if grp == 0 else [])
                te = plan.op("A", fe, deps=d, inc="A")
                last[f"pm{b}"] = te
            last["ao"] = te
            last[f"ctxdone_{cp}"] = te
            tsq = plan.op("A", lambda a: a.activation(
                out=sq_sb.ap(), in_=ao_sb.ap(), func=AF.Square),
                deps=[last["ao"]] + dep("sq_free"), inc="A")
            b3, tm3 = ln_stats_mms(ao_sb, last["ao"], tsq)
            t3 = ln_stats_math(b3, tm3)
            tbc3 = ln_bcast(t3, ab3_sb, "ab3free")
            a3 = ab3_sb.ap()[:, 0:T].unsqueeze(1).broadcast_to([128, NFT, T])
            b3r = ab3_sb.ap()[:, T:2 * T].unsqueeze(1).broadcast_to([128, NFT, T])
            ao3 = ao_sb.ap().rearrange("p (f t) -> p f t", f=NFT)
            sq3 = sq_sb.ap().rearrange("p (f t) -> p f t", f=NFT)
            x23 = x2c_sb.ap().rearrange("p (f t) -> p f t", f=NFT)
            xb3 = xb_sb[cp % 3].ap().rearrange("p (f t) -> p f t", f=NFT)
            tv = plan.op("V", lambda v, sq3=sq3, ao3=ao3, a3=a3: v.tensor_tensor(
                out=sq3, in0=ao3, in1=a3, op=ALU.mult),
                deps=[tbc3, tm3], inc="V")
            tv = plan.op("V", lambda v, sq3=sq3, b3r=b3r: v.tensor_tensor(
                out=sq3, in0=sq3, in1=b3r, op=ALU.subtract), deps=[tv],
                inc="V")
            tv = plan.op("V", lambda v, x23=x23, xb3=xb3, sq3=sq3: v.tensor_tensor(
                out=x23, in0=xb3, in1=sq3, op=ALU.add),
                deps=[tv] + dep("x2c_free") + [last[f"xb{cp}"]], inc="V")
            last["ao_free"] = tv
            last["ab3free"] = tv
            last[f"x2_{cp}"] = tv
            tsq = plan.op("A", lambda a: a.activation(
                out=sq_sb.ap(), in_=x2c_sb.ap(), func=AF.Square),
                deps=[tv], inc="A")
            b2, tm2 = ln_stats_mms(x2c_sb, tv, tsq)
            last["sq_free"] = tm2
            t2 = ln_stats_math(b2, tm2)
            cl2 = cp % CPW
            tcp = plan.op("V", lambda v, cl2=cl2: v.tensor_copy(
                out=a2b2_w.ap()[0:1, cl2 * T:(cl2 + 1) * T],
                in_=ab_row.ap()[0:1, 0:T]), deps=[t2] + dep("a2b2free"),
                inc="V")
            tcp = plan.op("V", lambda v, cl2=cl2: v.tensor_copy(
                out=a2b2_w.ap()[0:1, W + cl2 * T:W + (cl2 + 1) * T],
                in_=ab_row.ap()[0:1, T:2 * T]), deps=[tcp], inc="V")
            last["abrow_free"] = tcp
            if cl2 == CPW - 1:
                last[f"a2b2_{cp // CPW}"] = tcp
            def d_s(s, cp=cp):
                dst = x2h.rearrange("(fp p) tok -> p fp tok", p=128)[
                    :, :, cp * T:(cp + 1) * T]
                return s.dma_start(out=dst, in_=x2c_sb.ap().rearrange(
                    "p (f t) -> p f t", f=NFT))
            ts = plan.op("S", d_s, deps=[tv], inc="Dxw")
            last["x2c_free"] = ts
            if cp % CPW == CPW - 1:
                last[f"spill_w{cp // CPW}"] = ts

        if 0 <= c - 1 < NC and (c - 1) % CPW == CPW - 1:
            emit_xn2prep((c - 1) // CPW)

        if 3 <= c + 2 < NC:
            dma_xb(c + 2)

        if 0 <= c + 1 < NC:
            cn = c + 1
            if "prep" in DBG_SKIP:
                continue
            tsq = plan.op("A", lambda a, cn=cn: a.activation(
                out=sq_sb.ap(), in_=xb_sb[cn % 3].ap(), func=AF.Square),
                deps=[last[f"xb{cn}"]] + dep("sq_free"), inc="A")
            b1, tm1 = ln_stats_mms(xb_sb[cn % 3], last[f"xb{cn}"], tsq)
            last["sq_free"] = tm1
            t1 = ln_stats_math(b1, tm1)
            ln_bcast(t1, ab1_sb[cn % 2], f"ab1free{cn % 2}")
            last[f"ab1bc_{cn}"] = last["abrow_free"]

        if 0 <= c < NC:
            def emit_s(j):
                buf = es_sb[j % 2]
                qb = q_sb.ap()[:, j * T:(j + 1) * T].unsqueeze(1).broadcast_to(
                    [128, 32, T])
                k3 = kd_sb.ap().rearrange("p (e t) -> p e t", e=32)
                s3 = buf.ap().rearrange("p (e t) -> p e t", e=32)
                d = [last["q"], last["kd"]]
                if j >= 2:
                    d.append(last[f"ctxadd_{c}_{j - 2}"])
                elif c >= 1:
                    d.append(last[f"ctxadd_{c - 1}_{NJ - 1}"])
                tv = plan.op("V", lambda v, s3=s3, qb=qb, k3=k3:
                             v.tensor_tensor(out=s3, in0=qb, in1=k3,
                                             op=ALU.mult), deps=d, inc="V")
                if j == NJ - 1:
                    last["q_free"] = tv
                    last["kd_free"] = tv
                last[f"s_{c}_{j}"] = tv

            def emit_z(j):
                buf = es_sb[j % 2]
                tz = plan.op("V", lambda v, buf=buf: v.tensor_add(
                    out=zs_sb.ap(), in0=buf.ap()[:, 0:16 * T],
                    in1=buf.ap()[:, 16 * T:32 * T]),
                    deps=[last[f"exp_{c}_{j}"]], inc="V")
                ww = 8 * T
                while ww >= 2 * T:
                    tz = plan.op("V", lambda v, ww=ww: v.tensor_add(
                        out=zs_sb.ap()[:, 0:ww], in0=zs_sb.ap()[:, 0:ww],
                        in1=zs_sb.ap()[:, ww:2 * ww]), deps=[tz], inc="V")
                    ww //= 2
                tz = plan.op("V", lambda v, j=j: v.tensor_add(
                    out=zf_sb[j % 2].ap(), in0=zs_sb.ap()[:, 0:T],
                    in1=zs_sb.ap()[:, T:2 * T]),
                    deps=[tz] + dep(f"zffree{j % 2}"), inc="V")
                last[f"z_{c}_{j}"] = tz

            def emit_rz(j):
                ta = plan.op("A", lambda a, j=j: a.activation(
                    out=rz_sb[j % 2].ap(), in_=zf_sb[j % 2].ap(), func=AF.Ln),
                    deps=[last[f"z_{c}_{j}"]] + dep(f"rzfree{j % 2}"), inc="A")
                ta = plan.op("A", lambda a, j=j: a.activation(
                    out=rz_sb[j % 2].ap(), in_=rz_sb[j % 2].ap(), func=AF.Exp,
                    scale=-1.0), deps=[ta], inc="A")
                last[f"rz_{c}_{j}"] = ta
                last[f"zffree{j % 2}"] = ta

            def emit_exp(j):
                buf = es_sb[j % 2]
                ta = plan.op("A", lambda a, buf=buf: a.activation(
                    out=buf.ap(), in_=buf.ap(), func=AF.Exp, scale=0.125),
                    deps=[last[f"s_{c}_{j}"]], inc="A")
                last[f"exp_{c}_{j}"] = ta

            for j in range(NJ + 1):
                if j < NJ:
                    emit_s(j)
                if j >= 1:
                    emit_z(j - 1)
                    emit_rz(j - 1)
                if j >= 1:
                    emit_uP(c, j - 1)
                pull(1)
                if j < NJ:
                    emit_exp(j)
                pull(1)

        if 0 <= c + 1 < NC:
            cn = c + 1
            a1 = ab1_sb[cn % 2].ap()[:, 0:T].unsqueeze(1).broadcast_to(
                [128, NFT, T])
            b1r = ab1_sb[cn % 2].ap()[:, T:2 * T].unsqueeze(1).broadcast_to(
                [128, NFT, T])
            xbr = xb_sb[cn % 3].ap().rearrange("p (f t) -> p f t", f=NFT)
            xn1r = xn1_sb.ap().rearrange("p (f t) -> p f t", f=NFT)
            tv = plan.op("V", lambda v, xn1r=xn1r, xbr=xbr, a1=a1:
                         v.tensor_tensor(out=xn1r, in0=xbr, in1=a1,
                                         op=ALU.mult),
                         deps=[last[f"ab1bc_{cn}"], last[f"xb{cn}"]]
                         + dep("xn1_free"), inc="V")
            tv = plan.op("V", lambda v, xn1r=xn1r, b1r=b1r: v.tensor_tensor(
                out=xn1r, in0=xn1r, in1=b1r, op=ALU.subtract), deps=[tv],
                inc="V")
            last["xn1"] = tv
            last[f"ab1free{cn % 2}"] = tv
            if "qkv" not in DBG_SKIP:
                qkv_chunk(cn)
                last["xn1_free"] = last["v"]

        pull(len(atoms))

    out_final = plan.counts.get("Do", 0)

    sems = {name: nc.alloc_semaphore(f"s{name}") for name in plan.counts}

    qsems = {"G": ("Dcq", "Dco", "Dxb0", "Dxb1", "Dxb2",
                   "Dwg0", "Dwg1", "Dwu0", "Dwu1", "Dwd0", "Dwd1"),
             "S": ("Db", "Dxw", "Dr0", "Dr1", "Do",
                   "Dxr0", "Dxr1", "Dxr2", "Dxr3")}

    def run_stream(eng_handle, name):
        hw = {k: 0 for k in sems}
        for fn, waits, inc in plan.ops[name]:
            for s, v in waits:
                if v > hw[s]:
                    eng_handle.wait_ge(sems[s], v)
                    hw[s] = v
            inst = fn(eng_handle)
            if inc is not None:
                inst.then_inc(sems[inc], 16 if inc.startswith("D") else 1)
        for s in qsems.get(name, ()):
            v = plan.counts.get(s, 0)
            if v > hw.get(s, 0):
                eng_handle.wait_ge(sems[s], v)
                hw[s] = v
        return hw

    with nc.Block() as block:
        @block.tensor
        def _(tensor):
            run_stream(tensor, "T")

        @block.vector
        def _(vector):
            run_stream(vector, "V")

        @block.scalar
        def _(scalar):
            run_stream(scalar, "A")

        @block.gpsimd
        def _(gpsimd):
            run_stream(gpsimd, "G")

        @block.sync
        def _(sync):
            hw = run_stream(sync, "S")
            if out_final and hw.get("Do", 0) < out_final:
                sync.wait_ge(sems["Do"], out_final)

    return nc


def _ffn_atom_list(w):
    out = []
    for h in range(32):
        out.append(("gu", w, h))
    for dt in range(16):
        out.append(("down", w, 0, dt))
    for h in range(32, 64):
        out.append(("gu", w, h))
    for dt in range(16):
        out.append(("down", w, 1, dt))
    out.append(("ln4", w))
    out.append(("bc4", w))
    for k in range(16):
        out.append(("fin", w, k))
    return out


ATOMS = {w: _ffn_atom_list(w) for w in range(NW)}

_PROG = [None]


def _host_inputs(L0_attn_x, L0_attn_Wqkv, L0_attn_bqkv, L0_attn_Wout,
                 L0_attn_bout, L0_ffn_w_gate, L0_ffn_w_up, L0_ffn_w_down):
    xt = np.asarray(L0_attn_x, dtype=np.float32).reshape(-1, D)
    wgH = np.ascontiguousarray(
        np.asarray(L0_ffn_w_gate, np.float32).reshape(64, 128, 16, 128)
        .transpose(0, 3, 2, 1).reshape(HID, D))
    wuH = np.ascontiguousarray(
        np.asarray(L0_ffn_w_up, np.float32).reshape(64, 128, 16, 128)
        .transpose(0, 3, 2, 1).reshape(HID, D))
    wdH = np.ascontiguousarray(
        np.asarray(L0_ffn_w_down, np.float32).reshape(16, 128, 4, 16, 128)
        .transpose(2, 0, 4, 3, 1).reshape(HID, D))
    Wq = np.asarray(L0_attn_Wqkv, np.float32)
    qw = Wq[:, :, 0:64]
    kw = Wq[:, :, 64:128]
    vw = Wq[:, :, 128:192]
    qblk = np.zeros((128, 16, 128), np.float32)
    vblk = np.zeros((128, 16, 128), np.float32)
    for cp in range(16):
        qblk[0:64, cp, 0:64] = qw[2 * cp]
        qblk[64:128, cp, 64:128] = qw[2 * cp + 1]
        vblk[0:64, cp, 0:64] = vw[2 * cp]
        vblk[64:128, cp, 64:128] = vw[2 * cp + 1]
    kblk = np.zeros((128, 32, 128), np.float32)
    for e in range(32):
        r0 = 64 * (e % 2)
        kblk[r0:r0 + 64, e, 0:64] = kw[e]
        kblk[r0:r0 + 64, e, 64:128] = kw[e]
    wqkv = np.ascontiguousarray(np.concatenate(
        [qblk.reshape(128, 2048), kblk.reshape(128, 4096),
         vblk.reshape(128, 2048)], axis=1))
    Wo = np.asarray(L0_attn_Wout, np.float32)
    w2 = np.concatenate([Wo, Wo], axis=0)
    wout4 = np.ascontiguousarray(np.concatenate([w2, w2], axis=1))
    b3 = np.asarray(L0_attn_bqkv, np.float32).reshape(32, 3, 64)
    qpart = b3[:, 0, :]
    kpart = b3[:, 1, :]
    vpart = b3[:, 2, :]
    bq = np.ascontiguousarray(np.concatenate(
        [qpart[0::2].T, qpart[1::2].T], axis=0))
    bv = np.ascontiguousarray(np.concatenate(
        [vpart[0::2].T, vpart[1::2].T], axis=0))
    bk = np.ascontiguousarray(np.concatenate([kpart.T, kpart.T], axis=0))
    bo = np.ascontiguousarray(np.concatenate(
        [np.asarray(L0_attn_bout, np.float32)] * 2)[:, None])
    in_maps = []
    for i in range(NCORES):
        xs = xt[i * TOK:(i + 1) * TOK]
        xHc = np.ascontiguousarray(
            xs.reshape(NC, T, NFT, 128).transpose(0, 3, 2, 1)
            .reshape(TOK, D))
        in_maps.append({
            "xH": xHc,
            "wgH": wgH, "wuH": wuH, "wdH": wdH,
            "wqkv": wqkv, "wout4": wout4,
            "bq": bq, "bk": bk, "bv": bv, "bo": bo,
        })
    return in_maps


def kernel(L0_attn_x, L0_attn_Wqkv, L0_attn_bqkv, L0_attn_Wout, L0_attn_bout,
           L0_ffn_w_gate, L0_ffn_w_up, L0_ffn_w_down):
    in_maps = _host_inputs(L0_attn_x, L0_attn_Wqkv, L0_attn_bqkv, L0_attn_Wout,
                           L0_attn_bout, L0_ffn_w_gate, L0_ffn_w_up,
                           L0_ffn_w_down)
    if _PROG[0] is None:
        _PROG[0] = build_program()
    res = run_bass_kernel_spmd(_PROG[0], in_maps, core_ids=list(range(NCORES)))
    parts = [np.asarray(r["yT"]).T for r in res.results]
    out = np.concatenate(parts, axis=0).reshape(4, 4096, D)
    return np.ascontiguousarray(out, dtype=np.float32)


# version 19
# speedup vs baseline: 1.0412x; 1.0412x over previous
import numpy as np
import concourse.bass as bass
import concourse.mybir as mybir
from concourse.bass_utils import run_bass_kernel_spmd

F32 = mybir.dt.float32
BF16 = mybir.dt.bfloat16
AF = mybir.ActivationFunctionType
ALU = mybir.AluOpType

NCORES = 8
D = 2048
HID = 8192
TOK = 2048
T = 128
NC = TOK // T
W = 512
CPW = W // T
NW = TOK // W
NHT = HID // 128
NFT = D // 128
EPS = 1e-5
NJ = 16


class Plan:

    def __init__(self):
        self.ops = {"T": [], "V": [], "A": [], "G": [], "S": []}
        self.counts = {}

    def op(self, eng, fn, deps=(), inc=None):
        waits = {}
        for d in deps:
            if d is None:
                continue
            s, v = d
            waits[s] = max(waits.get(s, 0), v)
        tok = None
        if inc is not None:
            amt = 16 if inc.startswith("D") else 1
            self.counts[inc] = self.counts.get(inc, 0) + amt
            tok = (inc, self.counts[inc])
        self.ops[eng].append((fn, sorted(waits.items()), inc))
        return tok


DBG_SKIP = set()
DBG_X2 = False


def build_program(max_slot=NC):
    nc = bass.Bass("TRN2", target_bir_lowering=False)

    xH = nc.dram_tensor("xH", [TOK, D], F32, kind="ExternalInput").ap()
    wgT = nc.dram_tensor("wgH", [HID, D], F32, kind="ExternalInput").ap()
    wuT = nc.dram_tensor("wuH", [HID, D], F32, kind="ExternalInput").ap()
    wdT = nc.dram_tensor("wdH", [HID, D], F32, kind="ExternalInput").ap()
    wqkv_d = nc.dram_tensor("wqkv", [128, 64 * 128], F32, kind="ExternalInput").ap()
    wout_d = nc.dram_tensor("wout4", [128, 128], F32, kind="ExternalInput").ap()
    bq_d = nc.dram_tensor("bq", [128, 16], F32, kind="ExternalInput").ap()
    bk_d = nc.dram_tensor("bk", [128, 32], F32, kind="ExternalInput").ap()
    bv_d = nc.dram_tensor("bv", [128, 16], F32, kind="ExternalInput").ap()
    bo_d = nc.dram_tensor("bo", [128, 1], F32, kind="ExternalInput").ap()
    wgc = nc.dram_tensor("wgc", [HID, D], BF16, kind="Internal").ap()
    wuc = nc.dram_tensor("wuc", [HID, D], BF16, kind="Internal").ap()
    wdc = nc.dram_tensor("wdc", [HID, D], BF16, kind="Internal").ap()
    x2h = nc.dram_tensor("x2h", [D, TOK], BF16, kind="ExternalOutput" if DBG_X2 else "Internal").ap()
    yT = nc.dram_tensor("yT", [D, TOK], F32, kind="ExternalOutput").ap()

    A_ = nc.alloc_sbuf_tensor
    wqkv_sb = A_("wqkv_sb", [128, 64 * 128], BF16)
    wout_sb = A_("wout_sb", [128, 128], BF16)
    bq_sb = A_("bq_sb", [128, 16], F32)
    bk_sb = A_("bk_sb", [128, 32], F32)
    bv_sb = A_("bv_sb", [128, 16], F32)
    bo_sb = A_("bo_sb", [128, 1], F32)
    ones_c = A_("ones_c", [128, 1], BF16)
    ones_r = A_("ones_r", [1, 128], BF16)
    cst_sb = A_("cst_sb", [128, 2], F32)
    nc.const_aps.register(F32, EPS, cst_sb.ap()[:, 0:1])
    nc.const_aps.register(F32, 1.0, cst_sb.ap()[:, 1:2])

    wg_sb = [A_(f"wg_sb{i}", [128, NFT * 128], BF16) for i in range(2)]
    wu_sb = [A_(f"wu_sb{i}", [128, NFT * 128], BF16) for i in range(2)]
    wd_sb = [A_(f"wd_sb{i}", [128, 16 * 128], BF16) for i in range(2)]

    xb_sb = [A_(f"xb_sb{i}", [128, NFT * T], BF16) for i in range(3)]
    sq_sb = A_("sq_sb", [128, NFT * T], BF16)
    xn1_sb = A_("xn1_sb", [128, NFT * T], BF16)
    es_sb = [A_(f"es_sb{i}", [128, 32 * T], BF16) for i in range(2)]
    q_sb = A_("q_sb", [128, NJ * T], BF16)
    v_sb = A_("v_sb", [128, NJ * T], BF16)
    kd_sb = A_("kd_sb", [128, 32 * T], BF16)
    u_sb = A_("u_sb", [128, T], BF16)
    zs_sb = A_("zs_sb", [128, 16 * T], BF16)
    zf_sb = [A_(f"zf_sb{i}", [128, T], BF16) for i in range(2)]
    rz_sb = [A_(f"rz_sb{i}", [128, T], BF16) for i in range(2)]
    ctx_sb = A_("ctx_sb", [128, 32 * T], BF16)
    ao_sb = A_("ao_sb", [128, NFT * T], BF16)
    x2c_sb = A_("x2c_sb", [128, NFT * T], BF16)
    ab1_sb = [A_(f"ab1_sb{i}", [128, 2 * T], BF16) for i in range(2)]
    ab3_sb = A_("ab3_sb", [128, 2 * T], BF16)
    st_sb = A_("st_sb", [1, 3 * T], F32)
    ab_row = A_("ab_row", [1, 2 * T], BF16)
    a2b2_w = A_("a2b2_w", [1, 2 * W], BF16)
    st4_sb = A_("st4_sb", [1, 3 * W], F32)
    ab4_row = A_("ab4_row", [1, 2 * W], BF16)

    xn2_sb = A_("xn2_sb", [128, NFT * W], BF16)
    gated_sb = A_("gated_sb", [128, 32 * W], BF16)
    y2_sb = A_("y2_sb", [128, NFT * W], BF16)
    a2bc_sb = A_("a2bc_sb", [128, W], BF16)
    b2bc_sb = A_("b2bc_sb", [128, W], BF16)
    a4bc_sb = A_("a4bc_sb", [128, W], BF16)
    b4bc_sb = A_("b4bc_sb", [128, W], BF16)
    e1_sb = A_("e1_sb", [128, W], F32)
    m_sb = [A_(f"m_sb{i}", [128, W], BF16) for i in range(2)]
    t1_sb = [A_(f"t1_sb{i}", [128, W], BF16) for i in range(2)]
    sqw_sb = A_("sqw_sb", [128, W], BF16)
    x2s_sb = [A_(f"x2s_sb{i}", [128, W], BF16) for i in range(2)]
    x2scr_sb = [A_("x2scr0", [128, W], BF16)] * 2
    outs_sb = A_("outs_sb", [128, W], F32)

    P_ = nc.alloc_psum_tensor
    gps = [P_(f"gps{i}", [128, W], F32) for i in range(2)]
    ups = [P_(f"ups{i}", [128, W], F32) for i in range(2)]
    dps = [P_(f"dps{i}", [128, W], F32) for i in range(2)]
    mps = [P_(f"mps{i}", [128, 512], F32) for i in range(2)]

    plan = Plan()
    last = {}

    def dep(*keys):
        return [last.get(k) for k in keys]

    mbank_i = [0]

    def next_mbank():
        b = mbank_i[0] % 2
        mbank_i[0] += 1
        return b

    t = plan.op("G", lambda g: g.dma_start(out=wqkv_sb.ap(), in_=wqkv_d), inc="Dcq")
    last["wqkv"] = t
    t = plan.op("G", lambda g: g.dma_start(out=wout_sb.ap(), in_=wout_d), inc="Dco")
    last["wout"] = t
    for name, sb, dr in (("bq", bq_sb, bq_d), ("bk", bk_sb, bk_d),
                         ("bv", bv_sb, bv_d), ("bo", bo_sb, bo_d)):
        t = plan.op("S", lambda s, sb=sb, dr=dr: s.dma_start(out=sb.ap(), in_=dr),
                    inc="Db")
    last["bias"] = t
    t = plan.op("V", lambda v: v.memset(ones_c.ap(), 1.0), inc=None)
    t = plan.op("V", lambda v: v.memset(ones_r.ap(), 1.0), inc="V")
    last["ones"] = t
    t = plan.op("V", lambda v: v.memset(cst_sb.ap()[:, 0:1], EPS), inc=None)
    t = plan.op("V", lambda v: v.memset(cst_sb.ap()[:, 1:2], 1.0), inc="V")
    last["consts"] = t

    def dma_xb(c):
        buf = xb_sb[c % 3]

        def fn(g, c=c, buf=buf):
            return g.dma_start(out=buf.ap(),
                               in_=xH[c * 128:(c + 1) * 128, :])
        last[f"xb{c}"] = plan.op("G", fn, deps=dep(f"x2_{c - 3}"),
                                 inc=f"Dxb{c % 3}")

    def ln_stats_mms(src_sb, src_tok, sq_tok, extra=()):
        b = next_mbank()
        tk = None
        for ft in range(NFT):
            def fn(tn, ft=ft, b=b, src_sb=src_sb):
                return tn.matmul(out=mps[b].ap()[0:1, 0:T], lhsT=ones_c.ap(),
                                 rhs=src_sb.ap()[:, ft * T:(ft + 1) * T],
                                 start=(ft == 0), stop=(ft == NFT - 1),
                                 skip_group_check=True)
            d = (dep(f"pm{b}", "ones") + [src_tok] + list(extra)) if ft == 0 else ()
            tk = plan.op("T", fn, deps=d, inc=None)
        for ft in range(NFT):
            def fn(tn, ft=ft, b=b):
                return tn.matmul(out=mps[b].ap()[0:1, T:2 * T], lhsT=ones_c.ap(),
                                 rhs=sq_sb.ap()[:, ft * T:(ft + 1) * T],
                                 start=(ft == 0), stop=(ft == NFT - 1),
                                 skip_group_check=True)
            tk = plan.op("T", fn, deps=[sq_tok] if ft == 0 else (),
                         inc="T" if ft == NFT - 1 else None)
        return b, tk

    def ln_stats_math(bank, mm_tok, n=D):
        inv = 1.0 / n
        s0 = st_sb.ap()[0:1, 0:T]
        s1 = st_sb.ap()[0:1, T:2 * T]
        s2 = st_sb.ap()[0:1, 2 * T:3 * T]
        tm = plan.op("V", lambda v: v.tensor_scalar_mul(
            out=s0, in0=mps[bank].ap()[0:1, 0:T], scalar1=inv),
            deps=[mm_tok] + dep("st_free"), inc="V")
        tvv = plan.op("V", lambda v: v.tensor_scalar_mul(
            out=s1, in0=mps[bank].ap()[0:1, T:2 * T], scalar1=inv), inc="V")
        tmsq = plan.op("V", lambda v: v.tensor_mul(out=s2, in0=s0, in1=s0),
                       deps=[tm], inc="V")
        t = plan.op("V", lambda v: v.tensor_sub(out=s1, in0=s1, in1=s2),
                    deps=[tvv, tmsq], inc="V")
        last[f"pm{bank}"] = t
        ta = plan.op("A", lambda a: a.activation(
            out=s1, in_=s1, func=AF.Ln, bias=EPS),
            deps=[t] + dep("consts"), inc="A")
        ta = plan.op("A", lambda a: a.activation(
            out=s1, in_=s1, func=AF.Exp, scale=-0.5), deps=[ta], inc="A")
        tb = plan.op("V", lambda v: v.tensor_mul(out=s2, in0=s0, in1=s1),
                     deps=[ta, tm] + dep("abrow_free"), inc="V")
        t = plan.op("V", lambda v: v.tensor_copy(
            out=ab_row.ap()[0:1, 0:T], in_=s1), deps=[ta], inc="V")
        t = plan.op("V", lambda v: v.tensor_copy(
            out=ab_row.ap()[0:1, T:2 * T], in_=s2), deps=[tb], inc="V")
        last["st_free"] = t
        return t

    def ln_bcast(row_tok, dst_sb, free_key):
        b = next_mbank()
        tk = plan.op("T", lambda tn, b=b: tn.matmul(
            out=mps[b].ap()[:, 0:2 * T], lhsT=ones_r.ap(), rhs=ab_row.ap(),
            start=True, stop=True), deps=dep(f"pm{b}", "ones") + [row_tok],
            inc="T")
        te = plan.op("A", lambda a, b=b, dst_sb=dst_sb: a.activation(
            out=dst_sb.ap(), in_=mps[b].ap()[:, 0:2 * T], func=AF.Copy),
            deps=[tk] + dep(free_key), inc="A")
        last[f"pm{b}"] = te
        last["abrow_free"] = te
        return te

    def qkv_chunk(c):
        xn1_tok = last["xn1"]
        te = None
        for grp in range([] if "qkv_k" in DBG_SKIP else range(8)) if False else (range(0) if "qkv_k" in DBG_SKIP else range(8)):
            b = next_mbank()
            bank_dep = dep(f"pm{b}", "wqkv") + [xn1_tok]
            tks = []
            for i in range(4):
                e = grp * 4 + i
                def fn(tn, e=e, i=i, b=b):
                    return tn.matmul(
                        out=mps[b].ap()[:, i * T:(i + 1) * T],
                        lhsT=wqkv_sb.ap()[:, 2048 + e * 128:2048 + (e + 1) * 128],
                        rhs=xn1_sb.ap()[:, (e // 2) * T:(e // 2 + 1) * T],
                        start=True, stop=True)
                tks.append(plan.op("T", fn, deps=bank_dep if i == 0 else (),
                                   inc="T"))
            for i in range(4):
                e = grp * 4 + i
                d = [tks[3], last["bias"]]
                if grp == 0 and i == 0:
                    d += dep("kd_free")
                def fe(a, e=e, i=i, b=b):
                    return a.activation(out=kd_sb.ap()[:, e * T:(e + 1) * T],
                                        in_=mps[b].ap()[:, i * T:(i + 1) * T],
                                        func=AF.Identity,
                                        bias=bk_sb.ap()[:, e:e + 1])
                if "qkv_ev" not in DBG_SKIP:
                    te = plan.op("A", fe, deps=d, inc="A" if i == 3 else None)
                else:
                    te = tks[i]
            last[f"pm{b}"] = te
        last["kd"] = te
        for kind, dst, bias, coff in (("q", q_sb, bq_sb, 0),
                                      ("v", v_sb, bv_sb, 6144)):
            if f"qkv_{kind}" in DBG_SKIP:
                continue
            for grp in range(4):
                b = next_mbank()
                bank_dep = dep(f"pm{b}", "wqkv") + [xn1_tok]
                tks = []
                for i in range(4):
                    cp = grp * 4 + i
                    def fn(tn, cp=cp, i=i, b=b, coff=coff):
                        return tn.matmul(
                            out=mps[b].ap()[:, i * T:(i + 1) * T],
                            lhsT=wqkv_sb.ap()[:, coff + cp * 128:
                                              coff + (cp + 1) * 128],
                            rhs=xn1_sb.ap()[:, cp * T:(cp + 1) * T],
                            start=True, stop=True)
                    tks.append(plan.op("T", fn,
                                       deps=bank_dep if i == 0 else (),
                                       inc="T"))
                te = None
                for i in range(4):
                    cp = grp * 4 + i
                    d = [tks[3]]
                    if grp == 0 and i == 0:
                        d += dep(f"{kind}_free")
                    def fe(a, cp=cp, i=i, b=b, dst=dst, bias=bias):
                        return a.activation(out=dst.ap()[:, cp * T:(cp + 1) * T],
                                            in_=mps[b].ap()[:, i * T:(i + 1) * T],
                                            func=AF.Identity,
                                            bias=bias.ap()[:, cp:cp + 1])
                    if "qkv_ev" not in DBG_SKIP:
                        te = plan.op("A", fe, deps=d, inc="A" if i == 3 else None)
                    else:
                        te = tks[i]
                last[f"pm{b}"] = te
            last[kind] = te

    def emit_uP(c, j):
        buf = es_sb[j % 2]
        tu = plan.op("V", lambda v, j=j: v.tensor_mul(
            out=u_sb.ap(), in0=v_sb.ap()[:, j * T:(j + 1) * T],
            in1=rz_sb[j % 2].ap()),
            deps=[last[f"rz_{c}_{j}"], last["v"]] + dep("u_free"), inc="V")
        last[f"rzfree{j % 2}"] = tu
        if j == NJ - 1:
            last["v_free"] = tu
        ub = u_sb.ap().unsqueeze(1).broadcast_to([128, 32, T])
        s3 = buf.ap().rearrange("p (e t) -> p e t", e=32)
        tp = plan.op("V", lambda v, s3=s3, ub=ub: v.tensor_tensor(
            out=s3, in0=s3, in1=ub, op=ALU.mult), deps=[tu], inc="V")
        last["u_free"] = tp
        eng = "V"
        if j == 0:
            tg = plan.op(eng, lambda g, buf=buf: g.tensor_copy(
                out=ctx_sb.ap(), in_=buf.ap()),
                deps=[tp] + dep(f"ctxdone_{c - 1}"), inc=eng)
        else:
            tg = plan.op(eng, lambda g, buf=buf: g.tensor_add(
                out=ctx_sb.ap(), in0=ctx_sb.ap(), in1=buf.ap()),
                deps=[tp, last[f"ctxadd_{c}_{j - 1}"]], inc=eng)
        last[f"ctxadd_{c}_{j}"] = tg
        if j == NJ - 1:
            last[f"ctx_{c}"] = tg

    def emit_gu(w, h):
        pb = h % 2
        def d_g(g, h=h, pb=pb):
            return g.dma_start(out=wg_sb[pb].ap(),
                               in_=wgc[h * 128:(h + 1) * 128, :])
        tg = plan.op("S", d_g, deps=dep(f"wgfree{pb}", f"wcg{h // 16}"),
                     inc=f"Dwg{pb}")
        def d_u(g, h=h, pb=pb):
            return g.dma_start(out=wu_sb[pb].ap(),
                               in_=wuc[h * 128:(h + 1) * 128, :])
        tu_d = plan.op("S", d_u, deps=dep(f"wufree{pb}", f"wcu{h // 16}"),
                       inc=f"Dwu{pb}")
        first = [tg, last[f"xn2_{w}"]] + dep(f"gufree{pb}")
        tk = None
        for ft in range(NFT):
            def fn(tn, ft=ft, pb=pb):
                return tn.matmul(out=gps[pb].ap(),
                                 lhsT=wg_sb[pb].ap()[:, ft * 128:(ft + 1) * 128],
                                 rhs=xn2_sb.ap()[:, ft * W:(ft + 1) * W],
                                 start=(ft == 0), stop=(ft == NFT - 1))
            tk = plan.op("T", fn, deps=first if ft == 0 else (),
                         inc="T" if ft == NFT - 1 else None)
        tgate = tk
        last[f"wgfree{pb}"] = tgate
        for ft in range(NFT):
            def fn(tn, ft=ft, pb=pb):
                return tn.matmul(out=ups[pb].ap(),
                                 lhsT=wu_sb[pb].ap()[:, ft * 128:(ft + 1) * 128],
                                 rhs=xn2_sb.ap()[:, ft * W:(ft + 1) * W],
                                 start=(ft == 0), stop=(ft == NFT - 1))
            tk = plan.op("T", fn, deps=[tu_d] if ft == 0 else (),
                         inc="T" if ft == NFT - 1 else None)
        tup = tk
        last[f"wufree{pb}"] = tup
        if h == NHT - 1:
            last["xn2_free"] = tup
        ta = plan.op("A", lambda a, pb=pb: a.activation(
            out=e1_sb.ap(), in_=gps[pb].ap(), func=AF.Exp, scale=-1.0),
            deps=[tgate] + dep("e1free"), inc="A")
        ta = plan.op("A", lambda a: a.activation(
            out=e1_sb.ap(), in_=e1_sb.ap(), func=AF.Ln, bias=1.0),
            deps=[ta] + dep("consts"), inc="A")
        ta = plan.op("A", lambda a, pb=pb: a.activation(
            out=m_sb[pb].ap(), in_=e1_sb.ap(), func=AF.Exp, scale=-1.0),
            deps=[ta] + dep(f"mfree{pb}"), inc="A")
        last["e1free"] = ta
        tv = plan.op("V", lambda v, pb=pb: v.tensor_mul(
            out=t1_sb[pb].ap(), in0=m_sb[pb].ap(), in1=gps[pb].ap()),
            deps=[ta, tgate] + dep(f"t1free{pb}"), inc="V")
        last[f"mfree{pb}"] = tv
        hs = h % 32
        tv = plan.op("V", lambda v, hs=hs, pb=pb: v.tensor_mul(
            out=gated_sb.ap()[:, hs * W:(hs + 1) * W],
            in0=t1_sb[pb].ap(), in1=ups[pb].ap()),
            deps=[tup, tv] + dep(f"gatedfree{hs}"), inc="V")
        last[f"t1free{pb}"] = tv
        last[f"gufree{pb}"] = tv
        last[f"gated{hs}"] = tv

    def emit_down(w, hh, dt):
        pb = dt % 2
        tk = None
        for half in (0, 1):
            wb = (2 * dt + half) % 2
            def d_w(g, hh=hh, dt=dt, half=half, wb=wb):
                r0 = (((2 * hh + half) * 16 + dt)) * 128
                return g.dma_start(out=wd_sb[wb].ap(),
                                   in_=wdc[r0:r0 + 128, :])
            td = plan.op("S", d_w,
                         deps=dep(f"wdfree{wb}", f"wcd{2 * hh + half}"),
                         inc=f"Dwd{wb}")
            first = [td]
            if half == 0:
                first += [last.get(f"dfree{pb}")] + \
                    [last[f"gated{i}"] for i in range(32)]
            for sh in range(16):
                ht = 16 * half + sh
                def fn(tn, ht=ht, sh=sh, pb=pb, wb=wb, half=half):
                    return tn.matmul(
                        out=dps[pb].ap(),
                        lhsT=wd_sb[wb].ap()[:, sh * 128:(sh + 1) * 128],
                        rhs=gated_sb.ap()[:, ht * W:(ht + 1) * W],
                        start=(ht == 0), stop=(ht == 31))
                tk = plan.op("T", fn, deps=first if sh == 0 else (),
                             inc="T" if (sh == 15) else None)
            last[f"wdfree{wb}"] = tk
        if dt == 15:
            for i in range(32):
                last[f"gatedfree{i}"] = tk
        if hh == 0:
            te = plan.op("A", lambda a, dt=dt, pb=pb: a.activation(
                out=y2_sb.ap()[:, dt * W:(dt + 1) * W], in_=dps[pb].ap(),
                func=AF.Copy), deps=[tk] + dep(f"y2free{dt}"), inc="A")
        else:
            te = plan.op("V", lambda v, dt=dt, pb=pb: v.tensor_add(
                out=y2_sb.ap()[:, dt * W:(dt + 1) * W],
                in0=y2_sb.ap()[:, dt * W:(dt + 1) * W], in1=dps[pb].ap()),
                deps=[tk], inc="V")
            last[f"y2_{dt}"] = te
        last[f"dfree{pb}"] = te

    def emit_ln4(w):
        y2deps = [last[f"y2_{dt}"] for dt in range(NFT)]
        b1 = next_mbank()
        tk = None
        for dt in range(NFT):
            def fn(tn, dt=dt, b1=b1):
                return tn.matmul(out=mps[b1].ap()[0:1, 0:W], lhsT=ones_c.ap(),
                                 rhs=y2_sb.ap()[:, dt * W:(dt + 1) * W],
                                 start=(dt == 0), stop=(dt == NFT - 1),
                                 skip_group_check=True)
            d = (dep(f"pm{b1}", "ones") + y2deps) if dt == 0 else ()
            tk = plan.op("T", fn, deps=d, inc="T" if dt == NFT - 1 else None)
        sum_tok = tk
        b2 = next_mbank()
        tprev = None
        tsq = None
        for dt in range(NFT):
            d = [tprev, tsq] if tprev is not None else (
                y2deps + dep(f"pm{b2}"))
            tsq = plan.op("A", lambda a, dt=dt: a.activation(
                out=sqw_sb.ap(), in_=y2_sb.ap()[:, dt * W:(dt + 1) * W],
                func=AF.Square), deps=d, inc="A")
            def fn(tn, dt=dt, b2=b2):
                return tn.matmul(out=mps[b2].ap()[0:1, 0:W], lhsT=ones_c.ap(),
                                 rhs=sqw_sb.ap(), start=(dt == 0),
                                 stop=(dt == NFT - 1), skip_group_check=True)
            tprev = plan.op("T", fn, deps=[tsq], inc="T")
        sq_tok = tprev
        inv = 1.0 / D
        s0 = st4_sb.ap()[0:1, 0:W]
        s1 = st4_sb.ap()[0:1, W:2 * W]
        s2 = st4_sb.ap()[0:1, 2 * W:3 * W]
        tm = plan.op("V", lambda v, b1=b1: v.tensor_scalar_mul(
            out=s0, in0=mps[b1].ap()[0:1, 0:W], scalar1=inv),
            deps=[sum_tok, sq_tok], inc="V")
        tvv = plan.op("V", lambda v, b2=b2: v.tensor_scalar_mul(
            out=s1, in0=mps[b2].ap()[0:1, 0:W], scalar1=inv), inc="V")
        tmsq = plan.op("V", lambda v: v.tensor_mul(out=s2, in0=s0, in1=s0),
                       deps=[tm], inc="V")
        t = plan.op("V", lambda v: v.tensor_sub(out=s1, in0=s1, in1=s2),
                    deps=[tvv, tmsq], inc="V")
        last[f"pm{b1}"] = t
        last[f"pm{b2}"] = t
        ta = plan.op("A", lambda a: a.activation(
            out=s1, in_=s1, func=AF.Ln, bias=EPS),
            deps=[t] + dep("consts"), inc="A")
        ta = plan.op("A", lambda a: a.activation(
            out=s1, in_=s1, func=AF.Exp, scale=-0.5), deps=[ta], inc="A")
        tb = plan.op("V", lambda v: v.tensor_mul(out=s2, in0=s0, in1=s1),
                     deps=[ta, tm], inc="V")
        t = plan.op("V", lambda v: v.tensor_copy(
            out=ab4_row.ap()[0:1, 0:W], in_=s1), deps=[ta], inc="V")
        t = plan.op("V", lambda v: v.tensor_copy(
            out=ab4_row.ap()[0:1, W:2 * W], in_=s2), deps=[tb], inc="V")
        last["ab4row"] = t

    def emit_bc4(w):
        te = None
        for half, dst in ((0, a4bc_sb), (1, b4bc_sb)):
            b = next_mbank()
            tk = plan.op("T", lambda tn, half=half, b=b: tn.matmul(
                out=mps[b].ap()[:, 0:W], lhsT=ones_r.ap(),
                rhs=ab4_row.ap()[0:1, half * W:(half + 1) * W],
                start=True, stop=True), deps=dep(f"pm{b}") + [last["ab4row"]],
                inc="T")
            te = plan.op("A", lambda a, b=b, dst=dst: a.activation(
                out=dst.ap(), in_=mps[b].ap()[:, 0:W], func=AF.Copy),
                deps=[tk] + dep("ab4bcfree"), inc="A")
            last[f"pm{b}"] = te
        last["ab4bc"] = te

    def emit_fin(w, k):
        pb = k % 2
        def d_r(s, w=w, k=k, pb=pb):
            src = x2h.rearrange("(fp p) tok -> p fp tok", p=128)[
                :, k:k + 1, w * W:(w + 1) * W]
            return s.dma_start(out=x2s_sb[pb].ap().unsqueeze(1), in_=src)
        tr = plan.op("S", d_r,
                     deps=dep(f"x2sfree{pb}") + [last[f"spill_w{w}"]],
                     inc=f"Dr{pb}")
        y2r = y2_sb.ap()[:, k * W:(k + 1) * W]
        scr = x2scr_sb[pb].ap()
        tv = plan.op("V", lambda v, y2r=y2r, scr=scr: v.tensor_mul(
            out=scr, in0=y2r, in1=a4bc_sb.ap()),
            deps=[last["ab4bc"], last[f"y2_{k}"]] + dep("x2scrfree"),
            inc="V")
        tv = plan.op("V", lambda v, scr=scr: v.tensor_sub(
            out=scr, in0=scr, in1=b4bc_sb.ap()), deps=[tv], inc="V")
        tv = plan.op("V", lambda v, scr=scr, pb=pb: v.tensor_add(
            out=outs_sb.ap(), in0=x2s_sb[pb].ap(), in1=scr),
            deps=[tv, tr] + dep("outsfree"), inc="V")
        last[f"x2sfree{pb}"] = tv
        last["x2scrfree"] = tv
        last[f"y2free{k}"] = tv
        if k == NFT - 1:
            last["ab4bcfree"] = tv
        def d_wr(s, w=w, k=k):
            dst = yT.rearrange("(fp p) tok -> p fp tok", p=128)[
                :, k:k + 1, w * W:(w + 1) * W]
            return s.dma_start(out=dst, in_=outs_sb.ap().unsqueeze(1))
        tw = plan.op("S", d_wr, deps=[tv], inc="Do")
        last["outsfree"] = tw

    def emit_xn2prep(w):
        te = None
        for half, dst in ((0, a2bc_sb), (1, b2bc_sb)):
            b = next_mbank()
            tk = plan.op("T", lambda tn, half=half, b=b: tn.matmul(
                out=mps[b].ap()[:, 0:W], lhsT=ones_r.ap(),
                rhs=a2b2_w.ap()[0:1, half * W:(half + 1) * W],
                start=True, stop=True),
                deps=dep(f"pm{b}") + [last[f"a2b2_{w}"]], inc="T")
            te = plan.op("A", lambda a, b=b, dst=dst: a.activation(
                out=dst.ap(), in_=mps[b].ap()[:, 0:W], func=AF.Copy),
                deps=[tk] + dep("a2bcfree"), inc="A")
            last[f"pm{b}"] = te
        ab2bc = te
        last["a2b2free"] = te
        trs = []
        for k in range(NFT):
            def d_r(s, w=w, k=k):
                src = x2h.rearrange("(fp p) tok -> p fp tok", p=128)[
                    :, k:k + 1, w * W:(w + 1) * W]
                return s.dma_start(
                    out=xn2_sb.ap()[:, k * W:(k + 1) * W].unsqueeze(1),
                    in_=src)
            d = dep("xn2_free") + [last[f"spill_w{w}"]]
            if k >= 4:
                d.append(trs[k - 4])
            trs.append(plan.op("S", d_r, deps=d, inc=f"Dxr{k % 4}"))
        tv = None
        for k in range(NFT):
            xn2r = xn2_sb.ap()[:, k * W:(k + 1) * W]
            tv = plan.op("V", lambda v, xn2r=xn2r: v.tensor_mul(
                out=xn2r, in0=xn2r, in1=a2bc_sb.ap()),
                deps=[trs[k], ab2bc], inc="V")
            tv = plan.op("V", lambda v, xn2r=xn2r: v.tensor_sub(
                out=xn2r, in0=xn2r, in1=b2bc_sb.ap()), deps=[tv], inc="V")
        last["a2bcfree"] = tv
        last[f"xn2_{w}"] = tv

    dma_xb(0)
    dma_xb(1)
    dma_xb(2)
    for s in range(4):
        for name_, dst_, src_ in (("g", wgc, wgT), ("u", wuc, wuT),
                                  ("d", wdc, wdT)):
            def fcache(g, dst_=dst_, src_=src_, s=s):
                return g.dma_start(out=dst_[s * 2048:(s + 1) * 2048, :],
                                   in_=src_[s * 2048:(s + 1) * 2048, :])
            last[f"wc{name_}{s}"] = plan.op("G", fcache, inc=f"Dwc{name_}{s}")

    for c in range(-1, min(max_slot, NC) + 1):
        pw = (c // CPW) - 1 if 0 <= c < NC else (NW - 1 if c == NC else -1)
        atoms = []
        if 0 <= c < NC and pw >= 0:
            al = ATOMS[pw]
            n = (len(al) + CPW - 1) // CPW
            cl = c % CPW
            atoms = al[cl * n:(cl + 1) * n]
        elif c == NC:
            atoms = ATOMS[NW - 1]
        ai = [0]

        def pull(k):
            while k > 0 and ai[0] < len(atoms):
                a = atoms[ai[0]]
                ai[0] += 1
                k -= 1
                if a[0] == "gu":
                    emit_gu(a[1], a[2])
                elif a[0] == "down":
                    emit_down(a[1], a[2], a[3])
                elif a[0] == "ln4":
                    emit_ln4(a[1])
                elif a[0] == "bc4":
                    emit_bc4(a[1])
                elif a[0] == "fin":
                    emit_fin(a[1], a[2])

        if 0 <= c - 1 < NC:
            cp = c - 1
            ctx_tok = last[f"ctx_{cp}"]
            te = None
            for grp in range(4):
                b = next_mbank()
                bank_dep = dep(f"pm{b}", "wout") + [ctx_tok]
                tk = None
                fst = True
                for i in range(4):
                    mp = grp * 4 + i
                    for par in (0, 1):
                        m = 2 * mp + par
                        def fn(tn, m=m, i=i, b=b, par=par):
                            return tn.matmul(
                                out=mps[b].ap()[64 * par:64 * par + 64,
                                                i * T:(i + 1) * T],
                                lhsT=wout_sb.ap()[:, 64 * par:64 * par + 64],
                                rhs=ctx_sb.ap()[:, m * T:(m + 1) * T],
                                start=True, stop=True,
                                tile_position=(0, 64 * par),
                                skip_group_check=True)
                        tk = plan.op("T", fn, deps=bank_dep if fst else (),
                                     inc="T" if (i == 3 and par == 1) else None)
                        fst = False
                def fe(a, grp=grp, b=b):
                    return a.activation(
                        out=ao_sb.ap()[:, grp * 4 * T:(grp + 1) * 4 * T],
                        in_=mps[b].ap()[:, 0:4 * T], func=AF.Identity,
                        bias=bo_sb.ap()[:, 0:1])
                d = [tk] + (dep("ao_free") if grp == 0 else [])
                te = plan.op("A", fe, deps=d, inc="A")
                last[f"pm{b}"] = te
            last["ao"] = te
            last[f"ctxdone_{cp}"] = te
            tsq = plan.op("A", lambda a: a.activation(
                out=sq_sb.ap(), in_=ao_sb.ap(), func=AF.Square),
                deps=[last["ao"]] + dep("sq_free"), inc="A")
            b3, tm3 = ln_stats_mms(ao_sb, last["ao"], tsq)
            t3 = ln_stats_math(b3, tm3)
            tbc3 = ln_bcast(t3, ab3_sb, "ab3free")
            a3 = ab3_sb.ap()[:, 0:T].unsqueeze(1).broadcast_to([128, NFT, T])
            b3r = ab3_sb.ap()[:, T:2 * T].unsqueeze(1).broadcast_to([128, NFT, T])
            ao3 = ao_sb.ap().rearrange("p (f t) -> p f t", f=NFT)
            sq3 = sq_sb.ap().rearrange("p (f t) -> p f t", f=NFT)
            x23 = x2c_sb.ap().rearrange("p (f t) -> p f t", f=NFT)
            xb3 = xb_sb[cp % 3].ap().rearrange("p (f t) -> p f t", f=NFT)
            tv = plan.op("V", lambda v, sq3=sq3, ao3=ao3, a3=a3: v.tensor_tensor(
                out=sq3, in0=ao3, in1=a3, op=ALU.mult),
                deps=[tbc3, tm3], inc="V")
            tv = plan.op("V", lambda v, sq3=sq3, b3r=b3r: v.tensor_tensor(
                out=sq3, in0=sq3, in1=b3r, op=ALU.subtract), deps=[tv],
                inc="V")
            tv = plan.op("V", lambda v, x23=x23, xb3=xb3, sq3=sq3: v.tensor_tensor(
                out=x23, in0=xb3, in1=sq3, op=ALU.add),
                deps=[tv] + dep("x2c_free") + [last[f"xb{cp}"]], inc="V")
            last["ao_free"] = tv
            last["ab3free"] = tv
            last[f"x2_{cp}"] = tv
            tsq = plan.op("A", lambda a: a.activation(
                out=sq_sb.ap(), in_=x2c_sb.ap(), func=AF.Square),
                deps=[tv], inc="A")
            b2, tm2 = ln_stats_mms(x2c_sb, tv, tsq)
            last["sq_free"] = tm2
            t2 = ln_stats_math(b2, tm2)
            cl2 = cp % CPW
            tcp = plan.op("V", lambda v, cl2=cl2: v.tensor_copy(
                out=a2b2_w.ap()[0:1, cl2 * T:(cl2 + 1) * T],
                in_=ab_row.ap()[0:1, 0:T]), deps=[t2] + dep("a2b2free"),
                inc="V")
            tcp = plan.op("V", lambda v, cl2=cl2: v.tensor_copy(
                out=a2b2_w.ap()[0:1, W + cl2 * T:W + (cl2 + 1) * T],
                in_=ab_row.ap()[0:1, T:2 * T]), deps=[tcp], inc="V")
            last["abrow_free"] = tcp
            if cl2 == CPW - 1:
                last[f"a2b2_{cp // CPW}"] = tcp
            def d_s(s, cp=cp):
                dst = x2h.rearrange("(fp p) tok -> p fp tok", p=128)[
                    :, :, cp * T:(cp + 1) * T]
                return s.dma_start(out=dst, in_=x2c_sb.ap().rearrange(
                    "p (f t) -> p f t", f=NFT))
            ts = plan.op("S", d_s, deps=[tv], inc="Dxw")
            last["x2c_free"] = ts
            if cp % CPW == CPW - 1:
                last[f"spill_w{cp // CPW}"] = ts

        def xn2prep_if_due():
            if 0 <= c - 1 < NC and (c - 1) % CPW == CPW - 1:
                emit_xn2prep((c - 1) // CPW)
        if not (0 <= c < NC):
            xn2prep_if_due()

        if 3 <= c + 2 < NC:
            dma_xb(c + 2)

        if 0 <= c + 1 < NC:
            cn = c + 1
            if "prep" in DBG_SKIP:
                continue
            tsq = plan.op("A", lambda a, cn=cn: a.activation(
                out=sq_sb.ap(), in_=xb_sb[cn % 3].ap(), func=AF.Square),
                deps=[last[f"xb{cn}"]] + dep("sq_free"), inc="A")
            b1, tm1 = ln_stats_mms(xb_sb[cn % 3], last[f"xb{cn}"], tsq)
            last["sq_free"] = tm1
            t1 = ln_stats_math(b1, tm1)
            ln_bcast(t1, ab1_sb[cn % 2], f"ab1free{cn % 2}")
            last[f"ab1bc_{cn}"] = last["abrow_free"]

        if 0 <= c < NC:
            def emit_s(j):
                buf = es_sb[j % 2]
                qb = q_sb.ap()[:, j * T:(j + 1) * T].unsqueeze(1).broadcast_to(
                    [128, 32, T])
                k3 = kd_sb.ap().rearrange("p (e t) -> p e t", e=32)
                s3 = buf.ap().rearrange("p (e t) -> p e t", e=32)
                d = [last["q"], last["kd"]]
                if j >= 2:
                    d.append(last[f"ctxadd_{c}_{j - 2}"])
                elif c >= 1:
                    d.append(last[f"ctxadd_{c - 1}_{NJ - 1}"])
                tv = plan.op("V", lambda v, s3=s3, qb=qb, k3=k3:
                             v.tensor_tensor(out=s3, in0=qb, in1=k3,
                                             op=ALU.mult), deps=d, inc="V")
                if j == NJ - 1:
                    last["q_free"] = tv
                    last["kd_free"] = tv
                last[f"s_{c}_{j}"] = tv

            def emit_z(j):
                buf = es_sb[j % 2]
                tz = plan.op("V", lambda v, buf=buf: v.tensor_add(
                    out=zs_sb.ap(), in0=buf.ap()[:, 0:16 * T],
                    in1=buf.ap()[:, 16 * T:32 * T]),
                    deps=[last[f"exp_{c}_{j}"]], inc="V")
                ww = 8 * T
                while ww >= 2 * T:
                    tz = plan.op("V", lambda v, ww=ww: v.tensor_add(
                        out=zs_sb.ap()[:, 0:ww], in0=zs_sb.ap()[:, 0:ww],
                        in1=zs_sb.ap()[:, ww:2 * ww]), deps=[tz], inc="V")
                    ww //= 2
                tz = plan.op("V", lambda v, j=j: v.tensor_add(
                    out=zf_sb[j % 2].ap(), in0=zs_sb.ap()[:, 0:T],
                    in1=zs_sb.ap()[:, T:2 * T]),
                    deps=[tz] + dep(f"zffree{j % 2}"), inc="V")
                last[f"z_{c}_{j}"] = tz

            def emit_rz(j):
                ta = plan.op("A", lambda a, j=j: a.activation(
                    out=rz_sb[j % 2].ap(), in_=zf_sb[j % 2].ap(), func=AF.Ln),
                    deps=[last[f"z_{c}_{j}"]] + dep(f"rzfree{j % 2}"), inc="A")
                ta = plan.op("A", lambda a, j=j: a.activation(
                    out=rz_sb[j % 2].ap(), in_=rz_sb[j % 2].ap(), func=AF.Exp,
                    scale=-1.0), deps=[ta], inc="A")
                last[f"rz_{c}_{j}"] = ta
                last[f"zffree{j % 2}"] = ta

            def emit_exp(j):
                buf = es_sb[j % 2]
                ta = plan.op("A", lambda a, buf=buf: a.activation(
                    out=buf.ap(), in_=buf.ap(), func=AF.Exp, scale=0.125),
                    deps=[last[f"s_{c}_{j}"]], inc="A")
                last[f"exp_{c}_{j}"] = ta

            jp0 = 5 if c % CPW == 0 else 0
            for j in range(NJ + 1):
                if j < NJ:
                    emit_s(j)
                if j >= 1:
                    emit_z(j - 1)
                    emit_rz(j - 1)
                if j >= 1:
                    emit_uP(c, j - 1)
                if j >= jp0:
                    pull(1)
                if j < NJ:
                    emit_exp(j)
                if j >= jp0:
                    pull(1)
                if j == 4:
                    xn2prep_if_due()

        if 0 <= c + 1 < NC:
            cn = c + 1
            a1 = ab1_sb[cn % 2].ap()[:, 0:T].unsqueeze(1).broadcast_to(
                [128, NFT, T])
            b1r = ab1_sb[cn % 2].ap()[:, T:2 * T].unsqueeze(1).broadcast_to(
                [128, NFT, T])
            xbr = xb_sb[cn % 3].ap().rearrange("p (f t) -> p f t", f=NFT)
            xn1r = xn1_sb.ap().rearrange("p (f t) -> p f t", f=NFT)
            tv = plan.op("V", lambda v, xn1r=xn1r, xbr=xbr, a1=a1:
                         v.tensor_tensor(out=xn1r, in0=xbr, in1=a1,
                                         op=ALU.mult),
                         deps=[last[f"ab1bc_{cn}"], last[f"xb{cn}"]]
                         + dep("xn1_free"), inc="V")
            tv = plan.op("V", lambda v, xn1r=xn1r, b1r=b1r: v.tensor_tensor(
                out=xn1r, in0=xn1r, in1=b1r, op=ALU.subtract), deps=[tv],
                inc="V")
            last["xn1"] = tv
            last[f"ab1free{cn % 2}"] = tv
            if "qkv" not in DBG_SKIP:
                qkv_chunk(cn)
                last["xn1_free"] = last["v"]

        pull(len(atoms))

    out_final = plan.counts.get("Do", 0)

    sems = {name: nc.alloc_semaphore(f"s{name}") for name in plan.counts}

    qsems = {"G": tuple(["Dcq", "Dco", "Dxb0", "Dxb1", "Dxb2"]
                        + [f"Dwc{n}{s}" for n in "gud" for s in range(4)]),
             "S": ("Db", "Dxw", "Dr0", "Dr1", "Do",
                   "Dxr0", "Dxr1", "Dxr2", "Dxr3",
                   "Dwg0", "Dwg1", "Dwu0", "Dwu1", "Dwd0", "Dwd1")}

    def run_stream(eng_handle, name):
        hw = {k: 0 for k in sems}
        for fn, waits, inc in plan.ops[name]:
            for s, v in waits:
                if v > hw[s]:
                    eng_handle.wait_ge(sems[s], v)
                    hw[s] = v
            inst = fn(eng_handle)
            if inc is not None:
                inst.then_inc(sems[inc], 16 if inc.startswith("D") else 1)
        for s in qsems.get(name, ()):
            v = plan.counts.get(s, 0)
            if v > hw.get(s, 0):
                eng_handle.wait_ge(sems[s], v)
                hw[s] = v
        return hw

    with nc.Block() as block:
        @block.tensor
        def _(tensor):
            run_stream(tensor, "T")

        @block.vector
        def _(vector):
            run_stream(vector, "V")

        @block.scalar
        def _(scalar):
            run_stream(scalar, "A")

        @block.gpsimd
        def _(gpsimd):
            run_stream(gpsimd, "G")

        @block.sync
        def _(sync):
            hw = run_stream(sync, "S")
            if out_final and hw.get("Do", 0) < out_final:
                sync.wait_ge(sems["Do"], out_final)

    return nc


def _ffn_atom_list(w):
    out = []
    for h in range(32):
        out.append(("gu", w, h))
    for dt in range(16):
        out.append(("down", w, 0, dt))
    for h in range(32, 64):
        out.append(("gu", w, h))
    for dt in range(16):
        out.append(("down", w, 1, dt))
    out.append(("ln4", w))
    out.append(("bc4", w))
    for k in range(16):
        out.append(("fin", w, k))
    return out


ATOMS = {w: _ffn_atom_list(w) for w in range(NW)}

_PROG = [None]


def _host_inputs(L0_attn_x, L0_attn_Wqkv, L0_attn_bqkv, L0_attn_Wout,
                 L0_attn_bout, L0_ffn_w_gate, L0_ffn_w_up, L0_ffn_w_down):
    xt = np.asarray(L0_attn_x, dtype=np.float32).reshape(-1, D)
    wgH = np.ascontiguousarray(
        np.asarray(L0_ffn_w_gate, np.float32).reshape(64, 128, 16, 128)
        .transpose(0, 3, 2, 1).reshape(HID, D))
    wuH = np.ascontiguousarray(
        np.asarray(L0_ffn_w_up, np.float32).reshape(64, 128, 16, 128)
        .transpose(0, 3, 2, 1).reshape(HID, D))
    wdH = np.ascontiguousarray(
        np.asarray(L0_ffn_w_down, np.float32).reshape(16, 128, 4, 16, 128)
        .transpose(2, 0, 4, 3, 1).reshape(HID, D))
    Wq = np.asarray(L0_attn_Wqkv, np.float32)
    qw = Wq[:, :, 0:64]
    kw = Wq[:, :, 64:128]
    vw = Wq[:, :, 128:192]
    qblk = np.zeros((128, 16, 128), np.float32)
    vblk = np.zeros((128, 16, 128), np.float32)
    for cp in range(16):
        qblk[0:64, cp, 0:64] = qw[2 * cp]
        qblk[64:128, cp, 64:128] = qw[2 * cp + 1]
        vblk[0:64, cp, 0:64] = vw[2 * cp]
        vblk[64:128, cp, 64:128] = vw[2 * cp + 1]
    kblk = np.zeros((128, 32, 128), np.float32)
    for e in range(32):
        r0 = 64 * (e % 2)
        kblk[r0:r0 + 64, e, 0:64] = kw[e]
        kblk[r0:r0 + 64, e, 64:128] = kw[e]
    wqkv = np.ascontiguousarray(np.concatenate(
        [qblk.reshape(128, 2048), kblk.reshape(128, 4096),
         vblk.reshape(128, 2048)], axis=1))
    Wo = np.asarray(L0_attn_Wout, np.float32)
    w2 = np.concatenate([Wo, Wo], axis=0)
    wout4 = np.ascontiguousarray(np.concatenate([w2, w2], axis=1))
    b3 = np.asarray(L0_attn_bqkv, np.float32).reshape(32, 3, 64)
    qpart = b3[:, 0, :]
    kpart = b3[:, 1, :]
    vpart = b3[:, 2, :]
    bq = np.ascontiguousarray(np.concatenate(
        [qpart[0::2].T, qpart[1::2].T], axis=0))
    bv = np.ascontiguousarray(np.concatenate(
        [vpart[0::2].T, vpart[1::2].T], axis=0))
    bk = np.ascontiguousarray(np.concatenate([kpart.T, kpart.T], axis=0))
    bo = np.ascontiguousarray(np.concatenate(
        [np.asarray(L0_attn_bout, np.float32)] * 2)[:, None])
    in_maps = []
    for i in range(NCORES):
        xs = xt[i * TOK:(i + 1) * TOK]
        xHc = np.ascontiguousarray(
            xs.reshape(NC, T, NFT, 128).transpose(0, 3, 2, 1)
            .reshape(TOK, D))
        in_maps.append({
            "xH": xHc,
            "wgH": wgH, "wuH": wuH, "wdH": wdH,
            "wqkv": wqkv, "wout4": wout4,
            "bq": bq, "bk": bk, "bv": bv, "bo": bo,
        })
    return in_maps


def kernel(L0_attn_x, L0_attn_Wqkv, L0_attn_bqkv, L0_attn_Wout, L0_attn_bout,
           L0_ffn_w_gate, L0_ffn_w_up, L0_ffn_w_down):
    in_maps = _host_inputs(L0_attn_x, L0_attn_Wqkv, L0_attn_bqkv, L0_attn_Wout,
                           L0_attn_bout, L0_ffn_w_gate, L0_ffn_w_up,
                           L0_ffn_w_down)
    if _PROG[0] is None:
        _PROG[0] = build_program()
    res = run_bass_kernel_spmd(_PROG[0], in_maps, core_ids=list(range(NCORES)))
    parts = [np.asarray(r["yT"]).T for r in res.results]
    out = np.concatenate(parts, axis=0).reshape(4, 4096, D)
    return np.ascontiguousarray(out, dtype=np.float32)
